# version 01
# baseline (speedup 1.0000x reference)
import bisect
import contextlib
import numpy as np

import concourse.bass as bass
import concourse.mybir as mybir
from concourse.bass_utils import run_bass_kernel_spmd

F32 = mybir.dt.float32
BF16 = mybir.dt.bfloat16
AF = mybir.ActivationFunctionType
OP = mybir.AluOpType
AX = mybir.AxisListType

NCORE = 8
SEQ = 8192
DM = 1024
DI = 2048
DS = 16
NDM = DM // 128
NDC = DI // 128
NEXP = 8
T = 512
NCH = SEQ // T
LN_EPS = 1e-5

_NCH_BUILD = NCH
_TAPS = ()
_BUILD_STATS = {}
_TAP_SPECS = {
    "xn": ([128, NDM * T], F32),
    "xlnb": ([128, NDM * T], BF16),
    "sm": ([8, T], F32),
    "x_in": ([128, NDC * (T + 3)], BF16),
    "u": ([128, NDC * T], F32),
    "xssm": ([65, T], F32),
    "y": ([128, NDC * T], F32),
    "acc": ([128, NDM * T], F32),
    "ysb": ([128, NDC * T], BF16),
    "state": ([128, NDC * DS], F32),
}


class Buf:
    __slots__ = ("name", "writer", "readers")

    def __init__(self, name=""):
        self.name = name
        self.writer = None
        self.readers = []


class MK:
    def __init__(self, nc, sems):
        self.nc = nc
        self.sems = sems
        self.serial = {k: 0 for k in sems}
        self.covered = {k: 0 for k in sems}
        self.value = {k: 0 for k in sems}
        self.ms_serials = {k: [] for k in sems}
        self.ms_values = {k: [] for k in sems}
        self.last_inst = {k: None for k in sems}
        self.waited = {}
        self.pe_fp32_dirty = False
        self.stats = {"waits": 0, "milestones": 0, "ops": 0}

    def eng(self, key):
        return getattr(self.nc, key[2:] if key.startswith("q_") else key)

    def milestone(self, key):
        if key.startswith("q_"):
            return
        p = self.serial[key] - self.covered[key]
        if p == 0:
            return
        if key == "tensor" and not self.pe_fp32_dirty:
            self.last_inst[key].then_inc(self.sems[key], p)
        else:
            self.eng(key).drain().then_inc(self.sems[key], p)
            if key == "tensor":
                self.pe_fp32_dirty = False
        self.value[key] += p
        self.covered[key] = self.serial[key]
        self.ms_serials[key].append(self.serial[key])
        self.ms_values[key].append(self.value[key])
        self.stats["milestones"] += 1

    @staticmethod
    def _stream(key):
        return key[2:] if key.startswith("q_") else key

    def _wait(self, waiter, pkey, s):
        if pkey == waiter:
            return
        if pkey.startswith("q_"):
            tgt = 16 * s
        else:
            if self.covered[pkey] < s:
                self.milestone(pkey)
            i = bisect.bisect_left(self.ms_serials[pkey], s)
            tgt = self.ms_values[pkey][i]
        wd = self.waited.setdefault(self._stream(waiter), {})
        if wd.get(pkey, 0) < tgt:
            self.eng(waiter).wait_ge(self.sems[pkey], tgt)
            wd[pkey] = tgt
            self.stats["waits"] += 1

    def op(self, key, fn, reads=(), writes=(), accum=False, fp32_mm=False):
        best = {}

        def add(tok):
            if tok is not None and best.get(tok[0], 0) < tok[1]:
                best[tok[0]] = tok[1]

        for b in reads:
            add(b.writer)
        for b in writes:
            add(b.writer)
            if not accum:
                for t in b.readers:
                    add(t)
        for pk in sorted(best):
            self._wait(key, pk, best[pk])
        inst = fn(self.eng(key))
        self.last_inst[key] = inst
        self.serial[key] += 1
        self.stats["ops"] += 1
        if fp32_mm:
            self.pe_fp32_dirty = True
        if key.startswith("q_"):
            inst.then_inc(self.sems[key], 16)
            self.value[key] = 16 * self.serial[key]
            self.covered[key] = self.serial[key]
        tok = (key, self.serial[key])
        for b in writes:
            b.writer = tok
            if not accum:
                b.readers = []
        for b in reads:
            b.readers.append(tok)
        return tok

    def finish(self, final_waiter="gpsimd"):
        for k in self.sems:
            self.milestone(k)
        for k in self.sems:
            if self.value[k] > 0 and k != final_waiter:
                self.eng(final_waiter).wait_ge(self.sems[k], self.value[k])


def build_program():
    nch = _NCH_BUILD
    nc = bass.Bass("TRN2", target_bir_lowering=False)

    def din(name, shape, dt=F32):
        return nc.dram_tensor(name, shape, dt, kind="ExternalInput").ap()

    d = {}
    d["x"] = din("x", [DM, SEQ])
    d["inx_f"] = din("inx_f", [DM, DI])
    d["inz_f"] = din("inz_f", [DM, DI])
    d["conv_f"] = din("conv_f", [4, DI, DI])
    d["out_f"] = din("out_f", [DI, DM])
    d["shg_f"] = din("shg_f", [DM, DI])
    d["shu_f"] = din("shu_f", [DM, DI])
    d["shd_f"] = din("shd_f", [DI, DM])
    d["rtg_f"] = din("rtg_f", [NEXP, DM, DI])
    d["rtu_f"] = din("rtu_f", [NEXP, DM, DI])
    d["rtd_f"] = din("rtd_f", [NEXP, DI, DM])
    d["xprojT"] = din("xprojT", [128, NDC * 65])
    d["gate_sb"] = din("gate_sb", [128, NDM * 8])
    d["dtw"] = din("dtw", [128, NDC])
    d["dtb"] = din("dtb", [128, NDC])
    d["convb"] = din("convb", [128, NDC])
    d["normw"] = din("normw", [128, NDM])
    d["normb"] = din("normb", [128, NDM])
    d["id65"] = din("id65", [65, 65])
    d["id128"] = din("id128", [128, 128])
    d["lt8"] = din("lt8", [8, 8])
    d["ones1"] = din("ones1", [65, 128])
    d["onesc"] = din("onesc", [128, 1])
    d["cvec"] = din("cvec", [128, 2])
    out_fm = nc.dram_tensor("out_fm", [DM, SEQ], F32, kind="ExternalOutput").ap()

    w = {}
    w["inx"] = nc.dram_tensor("w_inx", [DM, DI], BF16, kind="Internal").ap()
    w["inz"] = nc.dram_tensor("w_inz", [DM, DI], BF16, kind="Internal").ap()
    w["conv"] = nc.dram_tensor("w_conv", [4, DI, DI], BF16, kind="Internal").ap()
    w["out"] = nc.dram_tensor("w_out", [DI, DM], BF16, kind="Internal").ap()
    w["shg"] = nc.dram_tensor("w_shg", [DM, DI], BF16, kind="Internal").ap()
    w["shu"] = nc.dram_tensor("w_shu", [DM, DI], BF16, kind="Internal").ap()
    w["shd"] = nc.dram_tensor("w_shd", [DI, DM], BF16, kind="Internal").ap()
    w["rtg"] = nc.dram_tensor("w_rtg", [NEXP, DM, DI], BF16, kind="Internal").ap()
    w["rtu"] = nc.dram_tensor("w_rtu", [NEXP, DM, DI], BF16, kind="Internal").ap()
    w["rtd"] = nc.dram_tensor("w_rtd", [NEXP, DI, DM], BF16, kind="Internal").ap()

    taps = {}
    for tname in _TAPS:
        tshape, tdt = _TAP_SPECS[tname]
        taps[tname] = nc.dram_tensor("tap_" + tname, tshape, tdt,
                                     kind="ExternalOutput").ap()

    stack = contextlib.ExitStack()

    def sbt(name, shape, dt=F32):
        return stack.enter_context(nc.sbuf_tensor(name, shape, dt))

    def pst(name, shape):
        return stack.enter_context(nc.psum_tensor(name, shape, F32))

    with stack:
        S = {}
        S["xproj"] = sbt("c_xproj", [128, NDC * 65])
        S["gate"] = sbt("c_gate", [128, NDM * 8])
        S["dtw"] = sbt("c_dtw", [128, NDC])
        S["dtb"] = sbt("c_dtb", [128, NDC])
        S["dtw_n"] = sbt("c_dtw_n", [128, NDC])
        S["dtb_n"] = sbt("c_dtb_n", [128, NDC])
        S["convb"] = sbt("c_convb", [128, NDC])
        S["normw"] = sbt("c_normw", [128, NDM])
        S["normb"] = sbt("c_normb", [128, NDM])
        S["id65"] = sbt("c_id65", [65, 65])
        S["id128"] = sbt("c_id128", [128, 128])
        S["lt8"] = sbt("c_lt8", [8, 8])
        S["ones1"] = sbt("c_ones1", [65, 128])
        S["onesc"] = sbt("c_onesc", [128, 1])
        S["cvec"] = sbt("c_cvec", [128, 2])
        S["state"] = sbt("state", [128, NDC * DS])
        S["tail"] = sbt("tail", [128, NDC * 3], BF16)

        S["xn"] = sbt("xn", [128, NDM * T])
        S["xlnb"] = sbt("xlnb", [128, NDM * T], BF16)
        S["r1"] = sbt("r1", [128, NDC * T])
        S["u"] = sbt("u", [128, NDC * T])
        S["delta"] = sbt("delta_bc", [128, T])
        S["decay"] = sbt("decay_scr", [128, 2 * T])
        S["bu"] = sbt("bu_scr", [128, 2 * T])
        S["h"] = sbt("h_scr", [128, 2 * T])
        S["xssm"] = sbt("xssm", [65, T])
        S["rowsA"] = sbt("rowsA", [128, 2 * T])
        S["rowsB"] = sbt("rowsB", [128, 2 * T])
        S["lgfm"] = sbt("lg_fm", [8, T])
        S["eqf"] = sbt("eqf", [8, T])
        S["sm"] = sbt("sm", [8, T])
        S["lgtm"] = sbt("lg_tm", [128, 32])
        S["mxs"] = sbt("mxs", [128, 8])
        S["sz"] = sbt("sz_scr", [128, T])
        S["ysb"] = sbt("ysb", [128, NDC * T], BF16)
        S["xs0"] = sbt("xs0", [128, NDM * T], BF16)
        S["act"] = sbt("act", [128, NDC * T], BF16)
        S["sgb"] = sbt("sgb", [128, 4 * T], BF16)
        S["ub"] = sbt("ub", [128, 2 * T], BF16)
        S["acc"] = sbt("acc", [128, NDM * T])
        S["gs"] = sbt("g_scr", [128, T])
        S["ss"] = sbt("s_scr", [128, 2 * T])
        S["xr"] = sbt("xr", [128, 2 * T])
        S["wsb"] = sbt("wsb", [128, 8 * 512], BF16)

        P = {"m": pst("ps_main", [128, 4 * 512]),
             "b4": pst("ps_b4", [128, 512]),
             "b5": pst("ps_b5", [128, 512]),
             "s1": pst("ps_s1", [128, 512]),
             "s2": pst("ps_s2", [128, 512])}

        sems = {k: stack.enter_context(nc.semaphore("sem_" + k))
                for k in ["tensor", "vector", "scalar", "gpsimd", "sync",
                          "q_sync", "q_gpsimd"]}
        mk = MK(nc, sems)

        B = {k: Buf(k) for k in S}
        BP = {k: Buf("ps_" + k) for k in
              ["m0", "m1", "m2", "m3", "b4", "b5", "s1", "s2"]}
        Bw = {k: Buf("w_" + k) for k in w}
        Bxr = [Buf("xr_s0"), Buf("xr_s1")]
        Bstate = [Buf(f"st{i}") for i in range(NDC * DS)]

        r1f = S["r1"][:]
        r1b = S["r1"][:].bitcast(BF16)

        def xin_ap(i, lo, hi):
            base = i * (T + 3)
            return r1b[:, base + lo: base + hi]

        def yacc_ap(dc):
            return r1f[:, dc * T:(dc + 1) * T]

        def pm(j):
            return P["m"][:, j * 512:(j + 1) * 512]

        ID65 = S["id65"]
        ONES = S["ones1"]

        def selector(j):
            return ID65[:, j:j + 1].to_broadcast([65, 128])

        def sel8(e):
            return ID65[0:8, e:e + 1].to_broadcast([8, 128])

        def dma(q, out, in_, reads=(), writes=()):
            return mk.op(q, lambda e: e.dma_start(out=out, in_=in_),
                         reads=reads, writes=writes)

        def mm(out, lhsT, rhs, reads, wbuf, start, stop, fp32=False):
            return mk.op("tensor",
                         lambda e: e.matmul(out, lhsT, rhs, start=start, stop=stop),
                         reads=reads, writes=(wbuf,), accum=not start, fp32_mm=fp32)

        def act(outap, inap, func, reads, wbuf, bias=None, scale=None, accum=False):
            kw = {}
            if bias is not None:
                kw["bias"] = bias
            if scale is not None:
                kw["scale"] = scale
            return mk.op("scalar",
                         lambda e: e.activation(outap, inap, func, **kw),
                         reads=reads, writes=(wbuf,), accum=accum)

        def tt(outap, a, b_, op_, reads, wbuf, accum=False):
            return mk.op("vector", lambda e: e.tensor_tensor(outap, a, b_, op_),
                         reads=reads, writes=(wbuf,), accum=accum)

        def tap(name, srcap, reads):
            if name in taps:
                dma("q_gpsimd", taps[name], srcap, reads=reads)

        wslot = {"i": 0}
        wsbufs = [Buf(f"ws{i}") for i in range(8)]

        def wtile(src_ap, wkey):
            i = wslot["i"] % 8
            wslot["i"] += 1
            apo = S["wsb"][:, i * 512:(i + 1) * 512]
            dma("q_sync", apo, src_ap, reads=(Bw[wkey],), writes=(wsbufs[i],))
            return apo, wsbufs[i]

        def stream_mm(wdram, wkey, ntiles, colblk, rhs_fn, rhs_bufs, nb=4):
            for i in range(ntiles):
                wap, wb = wtile(wdram[i * 128:(i + 1) * 128,
                                      colblk * (nb * 128):(colblk + 1) * (nb * 128)],
                                wkey)
                rap, rb = rhs_fn(i)
                for j in range(nb):
                    mm(pm(j), wap[:, j * 128:(j + 1) * 128], rap,
                       (wb, rb), BP[f"m{j}"], start=(i == 0), stop=(i == ntiles - 1))
                mk.milestone("tensor")

        wsf_bufs = [Buf("wsf0"), Buf("wsf1")]
        pre = [("inx", d["inx_f"], w["inx"]), ("inz", d["inz_f"], w["inz"]),
               ("out", d["out_f"], w["out"]), ("shg", d["shg_f"], w["shg"]),
               ("shu", d["shu_f"], w["shu"]), ("shd", d["shd_f"], w["shd"])]
        for k in range(4):
            pre.append(("conv", d["conv_f"][k], w["conv"][k]))
        for e in range(NEXP):
            pre.append(("rtg", d["rtg_f"][e], w["rtg"][e]))
            pre.append(("rtu", d["rtu_f"][e], w["rtu"][e]))
            pre.append(("rtd", d["rtd_f"][e], w["rtd"][e]))
        wsb_f32 = S["wsb"][:].bitcast(F32)
        pp = 0
        for wkey, src, dst in pre:
            R, C = src.shape
            for r in range(R // 128):
                for cb in range(C // 512):
                    sl = pp % 2
                    pp += 1
                    fap = wsb_f32[:, 1024 + sl * 512:1024 + (sl + 1) * 512]
                    bap = S["wsb"][:, sl * 512:(sl + 1) * 512]
                    dma("q_sync", fap,
                        src[r * 128:(r + 1) * 128, cb * 512:(cb + 1) * 512],
                        writes=(wsf_bufs[sl],))
                    act(bap, fap, AF.Copy, reads=(wsf_bufs[sl],), wbuf=wsbufs[sl])
                    dma("q_gpsimd",
                        dst[r * 128:(r + 1) * 128, cb * 512:(cb + 1) * 512],
                        bap, reads=(wsbufs[sl],), writes=(Bw[wkey],))

        glue = ("scalar", mk.serial["scalar"])
        for i_ in (4, 5, 6, 7):
            wsbufs[i_].readers.append(glue)

        for nm, key in [("xproj", "xprojT"), ("gate", "gate_sb"), ("dtw", "dtw"),
                        ("dtb", "dtb"), ("convb", "convb"), ("normw", "normw"),
                        ("normb", "normb"), ("id65", "id65"), ("id128", "id128"),
                        ("lt8", "lt8"), ("ones1", "ones1"), ("onesc", "onesc"),
                        ("cvec", "cvec")]:
            dma("q_gpsimd", S[nm][:], d[key], writes=(B[nm],))
        mk.op("gpsimd", lambda e: e.memset(S["state"][:], 0.0), writes=(B["state"],))
        mk.op("gpsimd", lambda e: e.memset(S["tail"][:], 0.0), writes=(B["tail"],))
        mk.op("vector", lambda e: e.tensor_scalar(S["dtw_n"][:], S["dtw"][:],
                                                  -1.0, None, OP.mult),
              reads=(B["dtw"],), writes=(B["dtw_n"],))
        mk.op("vector", lambda e: e.tensor_scalar(S["dtb_n"][:], S["dtb"][:],
                                                  -1.0, None, OP.mult),
              reads=(B["dtb"],), writes=(B["dtb_n"],))

        rowsA, rowsB = S["rowsA"], S["rowsB"]
        BrA, BrB = B["rowsA"], B["rowsB"]
        RA0, RA1 = rowsA[0:1, 0:T], rowsA[0:1, T:2 * T]
        RB0, RB1 = rowsB[0:1, 0:T], rowsB[0:1, T:2 * T]
        MU2, RSTD2 = RB0, RB1
        SCORER = rowsA[64:65, 0:T]

        def ln_stats(src_fn, mu_ap, rstd_ap, mub, scr_ap, scrb, sq_ap, sqb):
            for i in range(NDM):
                rap, rb = src_fn(i)
                sq = S["bu"][:, (i % 2) * T:(i % 2 + 1) * T]
                act(sq, rap, AF.Square, reads=(rb,), wbuf=B["bu"])
                mm(P["s1"][0:1, :], S["onesc"][:], rap, (rb, B["onesc"]),
                   BP["s1"], start=(i == 0), stop=(i == NDM - 1), fp32=True)
                mm(P["s2"][0:1, :], S["onesc"][:], sq, (B["bu"], B["onesc"]),
                   BP["s2"], start=(i == 0), stop=(i == NDM - 1), fp32=True)
            act(mu_ap, P["s1"][0:1, :], AF.Identity,
                reads=(BP["s1"], B["cvec"]), wbuf=mub,
                scale=S["cvec"][0:1, 1:2], accum=True)
            act(scr_ap, P["s2"][0:1, :], AF.Identity,
                reads=(BP["s2"], B["cvec"]), wbuf=scrb,
                scale=S["cvec"][0:1, 1:2], accum=True)
            tt(sq_ap, mu_ap, mu_ap, OP.mult, (mub,), sqb, accum=True)
            tt(scr_ap, scr_ap, sq_ap, OP.subtract, (sqb, scrb), scrb, accum=True)
            act(scr_ap, scr_ap, AF.Sqrt, reads=(scrb, B["cvec"]), wbuf=scrb,
                bias=S["cvec"][0:1, 0:1], accum=True)
            mk.op("vector", lambda e: e.reciprocal(rstd_ap, scr_ap),
                  reads=(scrb,), writes=(mub,), accum=True)

        def bcast_rows(mu_ap, rstd_ap, mub):
            mm(P["b4"][:], ONES[0:1, :], mu_ap, (mub, B["ones1"]), BP["b4"],
               start=True, stop=True, fp32=True)
            mm(P["b5"][:], ONES[0:1, :], rstd_ap, (mub, B["ones1"]), BP["b5"],
               start=True, stop=True, fp32=True)

        def xln_derive(dst, dstbuf):
            for i in range(NDM):
                tmp = S["bu"][:, (i % 2) * T:(i % 2 + 1) * T]
                tt(tmp, S["xn"][:, i * T:(i + 1) * T], P["b4"][:], OP.subtract,
                   (B["xn"], BP["b4"]), B["bu"])
                tt(S["gs"][:], tmp, P["b5"][:], OP.mult, (B["bu"], BP["b5"]),
                   B["gs"])
                act(dst[:, i * T:(i + 1) * T], S["gs"][:], AF.Identity,
                    reads=(B["gs"], B["normw"], B["normb"]), wbuf=dstbuf,
                    scale=S["normw"][:, i:i + 1], bias=S["normb"][:, i:i + 1],
                    accum=(i > 0))
            mk.milestone("scalar")

        for c in range(nch):
            cs = slice(c * T, (c + 1) * T)

            def xload(i):
                sl = i % 2
                xsl = S["xr"][:, sl * T:(sl + 1) * T]
                dma("q_gpsimd", xsl, d["x"][i * 128:(i + 1) * 128, cs],
                    writes=(Bxr[sl],))
                return xsl, Bxr[sl]

            ln_stats(xload, RA0, RB1, BrA, RA1, BrA, RB0, BrB)
            bcast_rows(RA0, RB1, BrA)
            for i in range(NDM):
                xsl, xrb = xload(i)
                tmp = S["bu"][:, (i % 2) * T:(i % 2 + 1) * T]
                tt(tmp, xsl, P["b4"][:], OP.subtract, (xrb, BP["b4"]), B["bu"])
                tt(S["xn"][:, i * T:(i + 1) * T], tmp, P["b5"][:], OP.mult,
                   (B["bu"], BP["b5"]), B["xn"], accum=(i > 0))
            mk.milestone("vector")
            tap("xn", S["xn"][:], (B["xn"],))

            ln_stats(lambda i: (S["xn"][:, i * T:(i + 1) * T], B["xn"]),
                     MU2, RSTD2, BrB, RA0, BrA, RA1, BrA)
            bcast_rows(MU2, RSTD2, BrB)
            xln_derive(S["xlnb"], B["xlnb"])
            tap("xlnb", S["xlnb"][:], (B["xlnb"],))

            for i in range(NDM):
                mm(P["s1"][0:8, :], S["gate"][:, i * 8:(i + 1) * 8],
                   S["xn"][:, i * T:(i + 1) * T], (B["gate"], B["xn"]), BP["s1"],
                   start=(i == 0), stop=(i == NDM - 1), fp32=True)
            act(S["lgfm"][:], P["s1"][0:8, :], AF.Copy, reads=(BP["s1"],),
                wbuf=B["lgfm"])
            for j in range(4):
                mk.op("tensor",
                      lambda e, j=j: e.transpose(
                          P["s2"][0:128, 0:8],
                          S["lgfm"][0:8, j * 128:(j + 1) * 128], ID65[0:8, 0:8]),
                      reads=(B["lgfm"], B["id65"]), writes=(BP["s2"],), fp32_mm=True)
                act(S["lgtm"][:, j * 8:(j + 1) * 8], P["s2"][0:128, 0:8], AF.Copy,
                    reads=(BP["s2"],), wbuf=B["lgtm"], accum=(j > 0))
            for j in range(4):
                mk.op("vector",
                      lambda e, j=j: e.reduce_max(
                          S["mxs"][:, j:j + 1], S["lgtm"][:, j * 8:(j + 1) * 8],
                          AX.X),
                      reads=(B["lgtm"],), writes=(B["mxs"],), accum=(j > 0))
                act(S["mxs"][:, 4 + j:5 + j], S["mxs"][:, j:j + 1], AF.Sigmoid,
                    reads=(B["mxs"],), wbuf=B["mxs"], accum=True)
                mk.op("vector",
                      lambda e, j=j: e.tensor_scalar(
                          S["lgtm"][:, j * 8:(j + 1) * 8],
                          S["lgtm"][:, j * 8:(j + 1) * 8],
                          S["mxs"][:, j:j + 1], None, OP.is_ge),
                      reads=(B["mxs"], B["lgtm"]), writes=(B["lgtm"],), accum=True)
            for j in range(4):
                mk.op("tensor",
                      lambda e, j=j: e.transpose(
                          P["s2"][0:8, 0:128], S["lgtm"][:, j * 8:(j + 1) * 8],
                          S["id128"][:]),
                      reads=(B["lgtm"], B["id128"]), writes=(BP["s2"],),
                      fp32_mm=True)
                act(S["eqf"][:, j * 128:(j + 1) * 128], P["s2"][0:8, 0:128],
                    AF.Copy, reads=(BP["s2"],), wbuf=B["eqf"], accum=(j > 0))
                mk.op("tensor",
                      lambda e, j=j: e.transpose(
                          P["s2"][0:1, 0:128], S["mxs"][:, 4 + j:5 + j],
                          S["id128"][:]),
                      reads=(B["mxs"], B["id128"]), writes=(BP["s2"],),
                      fp32_mm=True)
                act(SCORER[:, j * 128:(j + 1) * 128] if True else None,
                    P["s2"][0:1, 0:128], AF.Copy, reads=(BP["s2"],), wbuf=BrB,
                    accum=True)
            mm(P["s2"][0:8, :], S["lt8"][:], S["eqf"][:], (B["lt8"], B["eqf"]),
               BP["s2"], start=True, stop=True, fp32=True)
            mk.op("vector",
                  lambda e: e.tensor_scalar(S["lgfm"][:], P["s2"][0:8, :], 0.5,
                                            None, OP.is_lt),
                  reads=(BP["s2"],), writes=(B["lgfm"],))
            tt(S["eqf"][:], S["eqf"][:], S["lgfm"][:], OP.mult,
               (B["lgfm"],), B["eqf"])
            mm(P["s2"][0:8, :], ONES[64:65, 0:8], SCORER,
               (B["ones1"], BrB), BP["s2"], start=True, stop=True, fp32=True)
            tt(S["sm"][:], S["eqf"][:], P["s2"][0:8, :], OP.mult,
               (B["eqf"], BP["s2"]), B["sm"])
            mk.milestone("vector")
            tap("sm", S["sm"][:], (B["sm"],))

            for i in range(NDC):
                act(xin_ap(i, 0, 3), S["tail"][:, i * 3:(i + 1) * 3], AF.Copy,
                    reads=(B["tail"],), wbuf=B["r1"], accum=(i > 0))
            for o4 in range(4):
                stream_mm(w["inx"], "inx", NDM, o4,
                          lambda i: (S["xlnb"][:, i * T:(i + 1) * T], B["xlnb"]),
                          None)
                for j in range(4):
                    dc = o4 * 4 + j
                    act(xin_ap(dc, 3, 3 + T), pm(j), AF.Copy,
                        reads=(BP[f"m{j}"],), wbuf=B["r1"], accum=True)
                mk.milestone("scalar")
            tap("x_in", r1b[:, 0:NDC * (T + 3)], (B["r1"],))

            for o4 in range(4):
                for k in range(4):
                    for i in range(NDC):
                        wap, wb = wtile(
                            w["conv"][k][i * 128:(i + 1) * 128,
                                         o4 * 512:(o4 + 1) * 512], "conv")
                        for j in range(4):
                            mm(pm(j), wap[:, j * 128:(j + 1) * 128],
                               xin_ap(i, k, k + T), (wb, B["r1"]), BP[f"m{j}"],
                               start=(k == 0 and i == 0),
                               stop=(k == 3 and i == NDC - 1))
                        mk.milestone("tensor")
                for j in range(4):
                    dc = o4 * 4 + j
                    act(S["u"][:, dc * T:(dc + 1) * T], pm(j), AF.Silu,
                        reads=(BP[f"m{j}"], B["convb"]), wbuf=B["u"],
                        bias=S["convb"][:, dc:dc + 1], accum=(dc > 0))
                    mm(P["s1"][0:65, :], S["xproj"][:, dc * 65:(dc + 1) * 65],
                       S["u"][:, dc * T:(dc + 1) * T], (B["xproj"], B["u"]),
                       BP["s1"], start=(dc == 0), stop=(dc == NDC - 1), fp32=True)
                mk.milestone("scalar")
            for i in range(NDC):
                act(S["tail"][:, i * 3:(i + 1) * 3], xin_ap(i, T, T + 3), AF.Copy,
                    reads=(B["r1"],), wbuf=B["tail"], accum=(i > 0))
            tap("u", S["u"][:], (B["u"],))
            act(S["xssm"][:], P["s1"][0:65, :], AF.Copy, reads=(BP["s1"],),
                wbuf=B["xssm"])
            tap("xssm", S["xssm"][:], (B["xssm"],))
            mm(P["b4"][:], ONES[64:65, :], S["xssm"][64:65, :],
               (B["xssm"], B["ones1"]), BP["b4"], start=True, stop=True,
               fp32=True)
            act(S["delta"][:], P["b4"][:], AF.Copy, reads=(BP["b4"],),
                wbuf=B["delta"])

            def emit_scan_unit(s):
                mm(P["b4"][:], selector(s), S["xssm"][:], (B["id65"], B["xssm"]),
                   BP["b4"], start=True, stop=True, fp32=True)
                mm(P["b5"][:], selector(32 + s), S["xssm"][:],
                   (B["id65"], B["xssm"]), BP["b5"], start=True, stop=True,
                   fp32=True)
                for dc in range(NDC):
                    dsl = S["decay"][:, (dc % 2) * T:(dc % 2 + 1) * T]
                    act(dsl, S["delta"][:], AF.Sigmoid,
                        reads=(B["delta"], B["dtw_n"], B["dtb_n"]),
                        wbuf=B["decay"],
                        scale=S["dtw_n"][:, dc:dc + 1],
                        bias=S["dtb_n"][:, dc:dc + 1])
                    busl = S["bu"][:, (dc % 2) * T:(dc % 2 + 1) * T]
                    tt(busl, S["u"][:, dc * T:(dc + 1) * T], P["b4"][:], OP.mult,
                       (B["u"], BP["b4"]), B["bu"])
                    hsl = S["h"][:, (dc % 2) * T:(dc % 2 + 1) * T]
                    mk.op("vector",
                          lambda e, hsl=hsl, busl=busl, dsl=dsl, dc=dc, s=s:
                          e.tensor_tensor_scan(
                              out=hsl, data0=dsl, data1=busl,
                              initial=S["state"][:, dc * DS + s:dc * DS + s + 1],
                              op0=OP.mult, op1=OP.add),
                          reads=(B["decay"], B["bu"], Bstate[dc * DS + s]),
                          writes=(B["h"],))
                    mk.op("gpsimd",
                          lambda e, hsl=hsl, dc=dc, s=s: e.tensor_copy(
                              S["state"][:, dc * DS + s:dc * DS + s + 1],
                              hsl[:, T - 1:T]),
                          reads=(B["h"],), writes=(Bstate[dc * DS + s],))
                    if s == 0:
                        tt(yacc_ap(dc), hsl, P["b5"][:], OP.mult,
                           (B["h"], BP["b5"]), B["r1"], accum=(dc > 0))
                    else:
                        tt(busl, hsl, P["b5"][:], OP.mult, (B["h"], BP["b5"]),
                           B["bu"])
                        tt(yacc_ap(dc), yacc_ap(dc), busl, OP.add,
                           (B["bu"],), B["r1"], accum=True)
                mk.milestone("vector")
                mk.milestone("scalar")

            def xs_build(e):
                rhs, rhsb = ((S["xs0"], B["xs0"]) if e % 2 == 0
                             else (S["xlnb"], B["xlnb"]))
                mm(P["s1"][:], sel8(e), S["sm"][:], (B["id65"], B["sm"]),
                   BP["s1"], start=True, stop=True, fp32=True)
                for i in range(NDM):
                    tt(rhs[:, i * T:(i + 1) * T], S["xn"][:, i * T:(i + 1) * T],
                       P["s1"][:], OP.mult, (B["xn"], BP["s1"]), rhsb,
                       accum=(i > 0))
                mk.milestone("vector")

            def ffn_unit(e):
                if e < 0:
                    wg_src, wu_src, rhs, rhsb = (w["shg"], w["shu"], S["xs0"],
                                                 B["xs0"])
                    gk, uk = "shg", "shu"
                else:
                    wg_src, wu_src = w["rtg"][e], w["rtu"][e]
                    gk, uk = "rtg", "rtu"
                    rhs, rhsb = ((S["xs0"], B["xs0"]) if e % 2 == 0
                                 else (S["xlnb"], B["xlnb"]))
                for o4 in range(4):
                    stream_mm(wg_src, gk, NDM, o4,
                              lambda i: (rhs[:, i * T:(i + 1) * T], rhsb), None)
                    for j in range(4):
                        act(S["sgb"][:, j * T:(j + 1) * T], pm(j), AF.Silu,
                            reads=(BP[f"m{j}"],), wbuf=B["sgb"],
                            accum=(j > 0))
                    mk.milestone("scalar")
                    stream_mm(wu_src, uk, NDM, o4,
                              lambda i: (rhs[:, i * T:(i + 1) * T], rhsb), None)
                    for j in range(4):
                        osub = o4 * 4 + j
                        usl = S["ub"][:, (j % 2) * T:(j % 2 + 1) * T]
                        act(usl, pm(j), AF.Copy, reads=(BP[f"m{j}"],),
                            wbuf=B["ub"])
                        tt(S["act"][:, osub * T:(osub + 1) * T],
                           S["sgb"][:, j * T:(j + 1) * T], usl, OP.mult,
                           (B["sgb"], B["ub"]), B["act"], accum=(osub > 0))
                    mk.milestone("vector")
                    mk.milestone("scalar")

            def ffn_down(e):
                wd_src = w["shd"] if e < 0 else w["rtd"][e]
                dk = "shd" if e < 0 else "rtd"
                first = (e < 0)
                for og4 in range(2):
                    stream_mm(wd_src, dk, NDC, og4,
                              lambda i: (S["act"][:, i * T:(i + 1) * T],
                                         B["act"]), None)
                    for j in range(4):
                        o = og4 * 4 + j
                        if first:
                            act(S["acc"][:, o * T:(o + 1) * T], pm(j), AF.Copy,
                                reads=(BP[f"m{j}"],), wbuf=B["acc"],
                                accum=(o > 0))
                        else:
                            tt(S["acc"][:, o * T:(o + 1) * T],
                               S["acc"][:, o * T:(o + 1) * T], pm(j), OP.add,
                               (BP[f"m{j}"],), B["acc"], accum=True)
                    mk.milestone("vector")
                    mk.milestone("scalar")

            scan_iter = iter(range(DS))

            def emit_some_scans(n):
                for _ in range(n):
                    s = next(scan_iter, None)
                    if s is not None:
                        emit_scan_unit(s)

            for i in range(NDM):
                act(S["xs0"][:, i * T:(i + 1) * T], S["xn"][:, i * T:(i + 1) * T],
                    AF.Copy, reads=(B["xn"],), wbuf=B["xs0"], accum=(i > 0))
            mk.milestone("scalar")
            for e in [-1] + list(range(NEXP)):
                if e >= 0:
                    xs_build(e)
                ffn_unit(e)
                emit_some_scans(1)
                ffn_down(e)
                emit_some_scans(1)
            emit_some_scans(DS)
            if c == 0:
                tap("state", S["state"][:], tuple(Bstate))
            tap("y", r1f[:, 0:NDC * T], (B["r1"],))
            tap("acc", S["acc"][:], (B["acc"],))

            bcast_rows(MU2, RSTD2, BrB)
            xln_derive(S["xlnb"], B["xlnb"])
            for dcg in range(4):
                stream_mm(w["inz"], "inz", NDM, dcg,
                          lambda i: (S["xlnb"][:, i * T:(i + 1) * T], B["xlnb"]),
                          None)
                for j in range(4):
                    dc = dcg * 4 + j
                    act(S["sz"][:], pm(j), AF.Silu, reads=(BP[f"m{j}"],),
                        wbuf=B["sz"])
                    tt(S["ysb"][:, dc * T:(dc + 1) * T], yacc_ap(dc), S["sz"][:],
                       OP.mult, (B["r1"], B["sz"]), B["ysb"], accum=(dc > 0))
                mk.milestone("vector")
                mk.milestone("scalar")
            tap("ysb", S["ysb"][:], (B["ysb"],))
            for og4 in range(2):
                stream_mm(w["out"], "out", NDC, og4,
                          lambda i: (S["ysb"][:, i * T:(i + 1) * T], B["ysb"]),
                          None)
                for j in range(4):
                    o = og4 * 4 + j
                    sl = o % 2
                    ssl = S["ss"][:, sl * T:(sl + 1) * T]
                    tt(ssl, pm(j), S["xn"][:, o * T:(o + 1) * T], OP.add,
                       (BP[f"m{j}"], B["xn"]), B["ss"])
                    act(S["gs"][:], S["acc"][:, o * T:(o + 1) * T], AF.Silu,
                        reads=(B["acc"],), wbuf=B["gs"])
                    tt(ssl, ssl, S["gs"][:], OP.mult, (B["gs"],), B["ss"],
                       accum=True)
                    xsl = S["xr"][:, sl * T:(sl + 1) * T]
                    dma("q_gpsimd", xsl, d["x"][o * 128:(o + 1) * 128, cs],
                        writes=(Bxr[sl],))
                    tt(ssl, ssl, xsl, OP.add, (Bxr[sl],), B["ss"], accum=True)
                    dma("q_gpsimd", out_fm[o * 128:(o + 1) * 128, cs], ssl,
                        reads=(B["ss"],))
                mk.milestone("vector")
                mk.milestone("scalar")

        mk.finish("gpsimd")
        _BUILD_STATS.clear()
        _BUILD_STATS.update(mk.stats)
        _BUILD_STATS["instrs"] = len(list(nc.all_instructions()))
    return nc


def _prep_inputs(L0_attn_x, L0_attn_norm_w, L0_attn_norm_b, L0_attn_in_proj_w,
                 L0_attn_conv_w, L0_attn_conv_b, L0_attn_x_proj_w, L0_attn_dt_w,
                 L0_attn_dt_b, L0_attn_out_proj_w, L0_ffn_gate_w, L0_ffn_sh_gate,
                 L0_ffn_sh_up, L0_ffn_sh_down, L0_ffn_rt_gate, L0_ffn_rt_up,
                 L0_ffn_rt_down):
    f = np.ascontiguousarray
    com = {}
    com["inx_f"] = f(np.asarray(L0_attn_in_proj_w, np.float32)[0:DI].T)
    com["inz_f"] = f(np.asarray(L0_attn_in_proj_w, np.float32)[DI:2 * DI].T)
    com["conv_f"] = f(np.transpose(np.asarray(L0_attn_conv_w, np.float32),
                                   (2, 1, 0)))
    com["out_f"] = f(np.asarray(L0_attn_out_proj_w, np.float32).T)
    com["shg_f"] = f(np.asarray(L0_ffn_sh_gate, np.float32))
    com["shu_f"] = f(np.asarray(L0_ffn_sh_up, np.float32))
    com["shd_f"] = f(np.asarray(L0_ffn_sh_down, np.float32))
    com["rtg_f"] = f(np.asarray(L0_ffn_rt_gate, np.float32))
    com["rtu_f"] = f(np.asarray(L0_ffn_rt_up, np.float32))
    com["rtd_f"] = f(np.asarray(L0_ffn_rt_down, np.float32))
    xpw = np.asarray(L0_attn_x_proj_w, np.float32)
    xp = np.zeros((DI, 65), np.float32)
    xp[:, 0:16] = xpw[1:17].T
    xp[:, 32:48] = xpw[17:33].T
    xp[:, 64] = xpw[0]
    com["xprojT"] = f(xp.reshape(NDC, 128, 65).transpose(1, 0, 2)
                      .reshape(128, NDC * 65))
    gw = np.asarray(L0_ffn_gate_w, np.float32)
    com["gate_sb"] = f(gw.reshape(NDM, 128, 8).transpose(1, 0, 2)
                       .reshape(128, NDM * 8))
    com["dtw"] = f(np.asarray(L0_attn_dt_w, np.float32).reshape(NDC, 128).T)
    com["dtb"] = f(np.asarray(L0_attn_dt_b, np.float32).reshape(NDC, 128).T)
    com["convb"] = f(np.asarray(L0_attn_conv_b, np.float32).reshape(NDC, 128).T)
    com["normw"] = f(np.asarray(L0_attn_norm_w, np.float32).reshape(NDM, 128).T)
    com["normb"] = f(np.asarray(L0_attn_norm_b, np.float32).reshape(NDM, 128).T)
    com["id65"] = np.eye(65, dtype=np.float32)
    com["id128"] = np.eye(128, dtype=np.float32)
    com["lt8"] = f(np.triu(np.ones((8, 8), np.float32), 1))
    com["ones1"] = np.ones((65, 128), np.float32)
    com["onesc"] = np.ones((128, 1), np.float32)
    cv = np.zeros((128, 2), np.float32)
    cv[:, 0] = LN_EPS
    cv[:, 1] = 1.0 / DM
    com["cvec"] = cv
    xs = np.asarray(L0_attn_x, np.float32)
    in_maps = []
    for cix in range(NCORE):
        m = dict(com)
        m["x"] = f(xs[cix].T)
        in_maps.append(m)
    return in_maps


def kernel(**inputs):
    in_maps = _prep_inputs(**inputs)
    nc = build_program()
    res = run_bass_kernel_spmd(nc, in_maps, core_ids=list(range(NCORE)))
    out = np.stack([np.ascontiguousarray(res.results[cix]["out_fm"].T)
                    for cix in range(NCORE)], axis=0)
    return out
